# version 25
# baseline (speedup 1.0000x reference)
import contextlib

import numpy as np
import ml_dtypes

import concourse.bass as bass
import concourse.mybir as mybir
from concourse.bass_utils import run_bass_kernel_spmd

F32 = mybir.dt.float32
BF16 = mybir.dt.bfloat16
F32R = mybir.dt.float32r
I8 = mybir.dt.int8
AF = mybir.ActivationFunctionType
ALU = mybir.AluOpType
AX = mybir.AxisListType.X

P = 128
DM = 2048
NE = 16
ND = 16
NJ = 64
WU = 128
TR = 2048
T = WU + TR
CH_OFF = (0, 512, 1024, 1536, 2048)
CH_LEN = (512, 512, 512, 512, 128)
NCH = 5
NQ = 4
QW = 512
NTT = 16
INV_D = 1.0 / DM
NG = 8
GJ = NJ // NG


class Ct:

    def __init__(self, ctx, nc, name, step=1):
        self.h = ctx.enter_context(nc.semaphore(name))
        self.n = 0
        self.step = step

    def inc(self, inst):
        inst.then_inc(self.h, self.step)
        self.n += self.step
        return self.n


class Ring:

    def __init__(self, ctx, nc, name, nslots):
        self.h = [ctx.enter_context(nc.semaphore(f"{name}{i}"))
                  for i in range(nslots)]
        self.cnt = [0] * nslots

    def inc(self, inst, slot=0):
        inst.then_inc(self.h[slot], 16)
        self.cnt[slot] += 16
        return self.cnt[slot]

    def wait(self, eng, slot=0):
        eng.wait_ge(self.h[slot], self.cnt[slot])

    def refill(self, eng, slot=0):
        if self.cnt[slot]:
            eng.wait_ge(self.h[slot], self.cnt[slot])

    def wait_all(self, eng):
        for i, h in enumerate(self.h):
            if self.cnt[i]:
                eng.wait_ge(h, self.cnt[i])


def build_nc():
    nc = bass.Bass("TRN2", target_bir_lowering=False, debug=False)

    x_full = nc.dram_tensor("x_full", [T, DM], F32, kind="ExternalInput").ap()
    winT5 = nc.dram_tensor("winT5", [NE, P, ND, P], F32, kind="ExternalInput").ap()
    winN5 = nc.dram_tensor("winN5", [ND, P, NE, P], F32, kind="ExternalInput").ap()
    wdtT3 = nc.dram_tensor("wdtT3", [NE, P, 32], F32, kind="ExternalInput").ap()
    bdt_in = nc.dram_tensor("bdt_in", [32, 1], F32, kind="ExternalInput").ap()
    logA_in = nc.dram_tensor("logA_in", [P, NE], F32, kind="ExternalInput").ap()
    woutT5 = nc.dram_tensor("woutT5", [NQ, P, ND, QW], F32, kind="ExternalInput").ap()
    wg5 = nc.dram_tensor("wg5", [NJ, P, ND, P], I8, kind="ExternalInput").ap()
    wu5 = nc.dram_tensor("wu5", [NJ, P, ND, P], I8, kind="ExternalInput").ap()
    wd5 = nc.dram_tensor("wd5", [NQ, P, NJ, QW], I8, kind="ExternalInput").ap()
    sg_in = nc.dram_tensor("sg_in", [P, NJ], F32, kind="ExternalInput").ap()
    su_in = nc.dram_tensor("su_in", [P, NJ], F32, kind="ExternalInput").ap()
    sd_in = nc.dram_tensor("sd_in", [1, DM], F32, kind="ExternalInput").ap()
    sel_in = nc.dram_tensor("sel_in", [NE, 64, P], BF16, kind="ExternalInput").ap()
    y_out = nc.dram_tensor("y_out", [TR, DM], F32, kind="ExternalOutput").ap()

    xn_scr = nc.dram_tensor("xn_scr", [T, DM], BF16, kind="Internal").ap()
    ys_scr = nc.dram_tensor("ys_scr", [NTT, P, NE, P], BF16, kind="Internal").ap()
    f_scr = nc.dram_tensor("f_scr", [NTT, P, NJ, P], BF16, kind="Internal").ap()

    SY, VE, AC, TE, GP = nc.sync, nc.vector, nc.scalar, nc.tensor, nc.gpsimd

    with contextlib.ExitStack() as top:
        E = top.enter_context

        r_dx = Ring(top, nc, "r_dx", 3)
        r_dxn = Ring(top, nc, "r_dxn", 3)
        r_dtr = Ring(top, nc, "r_dtr", 2)
        r_dwm = Ring(top, nc, "r_dwm", 3)
        s_dwc = Ct(top, nc, "s_dwc", 16)
        r_dwn = Ring(top, nc, "r_dwn", 5)
        r_dys = Ring(top, nc, "r_dys", 2)
        r_dwg = Ring(top, nc, "r_dwg", 3)
        r_dwu = Ring(top, nc, "r_dwu", 3)
        r_df = Ring(top, nc, "r_df", 2)
        r_dfl = Ring(top, nc, "r_dfl", 3)
        r_dyl = Ring(top, nc, "r_dyl", 3)
        r_dwd = Ring(top, nc, "r_dwd", NG)
        r_dwo = Ring(top, nc, "r_dwo", 2)
        r_dsd = Ring(top, nc, "r_dsd", 2)
        r_dx2 = Ring(top, nc, "r_dx2", 3)
        r_dout = Ring(top, nc, "r_dout", 4)
        c_pewc = Ct(top, nc, "c_pewc")
        c_evwc = Ct(top, nc, "c_evwc")
        c_lnv = Ct(top, nc, "c_lnv")
        c_xn = Ct(top, nc, "c_xn")
        c_pexp = Ct(top, nc, "c_pexp")
        c_evxp = Ct(top, nc, "c_evxp")
        c_pedtl = Ct(top, nc, "c_pedtl")
        c_z = Ct(top, nc, "c_z")
        c_dtc = Ct(top, nc, "c_dtc")
        c_dthl = Ct(top, nc, "c_dthl")
        c_exp = Ct(top, nc, "c_exp")
        c_scan = Ct(top, nc, "c_scan")
        c_ybf = Ct(top, nc, "c_ybf")
        c_pegu = Ct(top, nc, "c_pegu")
        c_ev = Ct(top, nc, "c_ev")
        c_f = Ct(top, nc, "c_f")
        c_peo = Ct(top, nc, "c_peo")
        c_add = Ct(top, nc, "c_add")
        c_wds = Ct(top, nc, "c_wds")
        e_dve = Ct(top, nc, "e_dve")
        e_act = Ct(top, nc, "e_act")
        e_gp = Ct(top, nc, "e_gp")

        bdt_c = E(nc.sbuf_tensor("bdt_c", [32, 1], F32))
        eps_c = E(nc.sbuf_tensor("eps_c", [P, 1], F32))
        one_c = E(nc.sbuf_tensor("one_c", [P, 1], F32))
        logA_s = E(nc.sbuf_tensor("logA_s", [P, NE], F32))
        negA_s = E(nc.sbuf_tensor("negA_s", [P, NE], F32))
        sg_c = E(nc.sbuf_tensor("sg_c", [P, NJ], F32))
        su_c = E(nc.sbuf_tensor("su_c", [P, NJ], F32))
        wc_sb = E(nc.sbuf_tensor("wc_sb", [P, NE * 32], BF16))
        stats = E(nc.sbuf_tensor("stats", [P, 48], F32))

        blk = E(nc.Block())

        def vchain(inst):
            VE.wait_ge(e_dve.h, e_dve.inc(inst))

        def achain(inst):
            AC.wait_ge(e_act.h, e_act.inc(inst))

        xnT_ctx = contextlib.ExitStack()
        xnT = xnT_ctx.enter_context(nc.sbuf_tensor("xnT", [P, NE * T], BF16))

        ab_ctx = contextlib.ExitStack()
        AE = ab_ctx.enter_context
        winT_b = [AE(nc.sbuf_tensor(f"winTb{i}", [P, ND * P], BF16)) for i in range(3)]
        xt_b = [AE(nc.sbuf_tensor(f"xtb{i}", [P, DM], F32)) for i in range(3)]
        xn_b = [AE(nc.sbuf_tensor(f"xnb{i}", [P, DM], BF16)) for i in range(3)]
        xsq_s = AE(nc.sbuf_tensor("xsq_s", [P, DM], BF16))
        xps_b = [AE(nc.sbuf_tensor(f"xpsb{i}", [P, 1152], F32)) for i in range(2)]
        z_s = AE(nc.sbuf_tensor("z_s", [P, 512], F32))
        zl_s = AE(nc.sbuf_tensor("zl_s", [P, 512], F32))
        zr_s = AE(nc.sbuf_tensor("zr_s", [P, 512], F32))
        dt_b = [AE(nc.sbuf_tensor(f"dtb{i}", [P, 512], F32)) for i in range(2)]
        a_b = [AE(nc.sbuf_tensor(f"ab{i}", [P, 1152], F32)) for i in range(2)]
        dthl_s = AE(nc.sbuf_tensor("dthl_s", [64, T], BF16))
        sel_sb = AE(nc.sbuf_tensor("sel_sb", [64, NE * P], BF16))
        ys_b = [AE(nc.sbuf_tensor(f"ysb{i}", [P, 1152], F32)) for i in range(2)]
        ybf_b = [AE(nc.sbuf_tensor(f"ybfb{i}", [P, 1152], BF16)) for i in range(2)]
        state_s = AE(nc.sbuf_tensor("state_s", [P, NE], F32))

        wn_ctx = contextlib.ExitStack()
        winN_b = [wn_ctx.enter_context(nc.sbuf_tensor(f"winNb{i}", [P, NE * P], BF16))
                  for i in range(5)]
        wdtT_s = wn_ctx.enter_context(nc.sbuf_tensor("wdtT_s", [P, NE * 32], BF16))

        pa_ctx = contextlib.ExitStack()
        ps_xp6 = pa_ctx.enter_context(nc.psum_tensor("ps_xp6", [P, 3072], F32))
        ps_xp = [ps_xp6[:, i * 512:(i + 1) * 512] for i in range(6)]
        ps_dtl = [pa_ctx.enter_context(nc.psum_tensor(f"psdtl{i}", [P, 512], F32))
                  for i in range(2)]

        consts_ready = {0: 0}
        PASS_CH = ((0, 1), (2, 3, 4))
        PASS_OFF = (0, 1024)
        PASS_LEN = (1024, 1152)
        wm_vals = {}
        pexp_at = {}
        exp_at = {}
        psum_free = {}
        psum_xp_bank = {}
        evxp_at = {}
        tile_state = {"i": 0}

        def emit_ln_front():
            i = tile_state["i"]
            tile_state["i"] += 1
            kb = i % 3
            row0 = i * P
            if i >= 3:
                AC.wait_ge(c_xn.h, i - 2)
            r_dx.refill(AC, kb)
            r_dx.inc(AC.dma_start(out=xt_b[kb][:, :],
                                  in_=x_full[row0:row0 + P, :]), kb)
            cb = 12 * (i % 3)
            r_dx.wait(VE, kb)
            if i >= 3:
                VE.wait_ge(c_xn.h, i - 2)
            vchain(VE.reduce_sum(out=stats[:, cb:cb + 1], in_=xt_b[kb][:, :], axis=AX))
            vchain(VE.scalar_tensor_tensor(out=xsq_s[:, :], in0=xt_b[kb][:, :],
                                           scalar=1.0, in1=xt_b[kb][:, :],
                                           op0=ALU.mult, op1=ALU.mult,
                                           accum_out=stats[:, cb + 1:cb + 2]))
            vchain(VE.tensor_scalar_mul(out=stats[:, cb + 2:cb + 3],
                                        in0=stats[:, cb:cb + 1], scalar1=-INV_D))
            vchain(VE.tensor_scalar_mul(out=stats[:, cb + 3:cb + 4],
                                        in0=stats[:, cb + 1:cb + 2], scalar1=INV_D))
            vchain(VE.tensor_mul(out=stats[:, cb + 4:cb + 5],
                                 in0=stats[:, cb + 2:cb + 3],
                                 in1=stats[:, cb + 2:cb + 3]))
            inst = VE.tensor_sub(out=stats[:, cb + 5:cb + 6],
                                 in0=stats[:, cb + 3:cb + 4],
                                 in1=stats[:, cb + 4:cb + 5])
            c_lnv.inc(inst)

        def emit_ln_back(i):
            kb = i % 3
            cb = 12 * (i % 3)
            row0 = i * P
            if i == 0:
                AC.wait_ge(e_dve.h, consts_ready[0])
            AC.wait_ge(c_lnv.h, i + 1)
            achain(AC.activation(out=stats[:, cb + 6:cb + 7],
                                 in_=stats[:, cb + 5:cb + 6], func=AF.Ln,
                                 bias=eps_c[:, :]))
            achain(AC.activation(out=stats[:, cb + 7:cb + 8],
                                 in_=stats[:, cb + 6:cb + 7], func=AF.Exp,
                                 scale=-0.5))
            achain(AC.mul(out=stats[:, cb + 8:cb + 9], in_=stats[:, cb + 2:cb + 3],
                          mul=stats[:, cb + 7:cb + 8]))
            if i >= 3:
                AC.wait_ge(r_dxn.h[kb], r_dxn.cnt[kb])
            inst = AC.activation(out=xn_b[kb][:, :], in_=xt_b[kb][:, :],
                                 func=AF.Identity, scale=stats[:, cb + 7:cb + 8],
                                 bias=stats[:, cb + 8:cb + 9])
            c_xn.inc(inst)
            SY.wait_ge(c_xn.h, c_xn.n)
            r_dxn.refill(SY, kb)
            r_dxn.inc(SY.dma_start(out=xn_scr[row0:row0 + P, :],
                                   in_=xn_b[kb][:, :]), kb)

        def psum_wait_dtl(eng, key_i):
            key = ("dtl", key_i)
            if key in psum_free:
                sem, val = psum_free[key]
                eng.wait_ge(sem, val)

        def emit_tr(c):
            off, ln = CH_OFF[c], CH_LEN[c]
            kc = c % 2
            r_dxn.wait_all(SY)
            if c >= 2:
                SY.wait_ge(c_pedtl.h, c - 1)
            r_dtr.refill(SY, kc)
            for dt in range(ND):
                r_dtr.inc(SY.dma_start(out=xnT[:, dt * T + off: dt * T + off + ln],
                                       in_=xn_scr[off:off + ln, dt * P:(dt + 1) * P],
                                       transpose=True), kc)

        def emit_dtl(c):
            off, ln = CH_OFF[c], CH_LEN[c]
            kc = c % 2
            kdt = c % 2
            r_dtr.wait(TE, kc)
            if c == 0:
                TE.wait_ge(c_evwc.h, ND)
            psum_wait_dtl(TE, kdt)
            for dt in range(ND):
                inst = TE.matmul(ps_dtl[kdt][0:32, 0:ln], wc_sb[:, dt * 32:(dt + 1) * 32],
                                 xnT[:, dt * T + off: dt * T + off + ln],
                                 start=(dt == 0), stop=(dt == ND - 1))
            c_pedtl.inc(inst)
            AC.wait_ge(c_pedtl.h, c_pedtl.n)
            if c >= 1:
                AC.wait_ge(c_dtc.h, c)
            v = c_z.inc(AC.activation(out=z_s[0:32, 0:ln], in_=ps_dtl[kdt][0:32, 0:ln],
                                      func=AF.Identity, bias=bdt_c[0:32, :]))
            psum_free[("dtl", kdt)] = (c_z.h, v)
            AC.wait_ge(c_z.h, c_z.n)
            achain(AC.activation(out=zl_s[0:32, 0:ln], in_=z_s[0:32, 0:ln], func=AF.Abs))
            achain(AC.activation(out=zl_s[0:32, 0:ln], in_=zl_s[0:32, 0:ln],
                                 func=AF.Exp, scale=-1.0))
            achain(AC.activation(out=zl_s[0:32, 0:ln], in_=zl_s[0:32, 0:ln],
                                 func=AF.Ln, bias=one_c[0:32, :]))
            achain(AC.activation(out=zr_s[0:32, 0:ln], in_=z_s[0:32, 0:ln], func=AF.Relu))
            VE.wait_ge(e_act.h, e_act.n)
            inst = VE.tensor_add(out=dt_b[kdt][0:32, 0:ln], in0=zr_s[0:32, 0:ln],
                                 in1=zl_s[0:32, 0:ln])
            c_dtc.inc(inst)
            VE.wait_ge(c_dtc.h, c_dtc.n)
            vchain(VE.tensor_copy(out=dthl_s[0:32, off:off + ln],
                                  in_=dt_b[kdt][0:32, 0:ln]))
            inst = VE.tensor_sub(out=dthl_s[32:64, off:off + ln],
                                 in0=dt_b[kdt][0:32, 0:ln],
                                 in1=dthl_s[0:32, off:off + ln])
            c_dthl.inc(inst)

        def emit_tr_dtl(c):
            emit_tr(c)
            emit_dtl(c)

        def emit_winT_load(gi2):
            et2 = gi2 % NE
            kw = gi2 % 3
            if gi2 >= 3:
                GP.wait_ge(c_pexp.h, pexp_at[gi2 - 3])
            r_dwm.refill(GP, kw)
            wm_vals[gi2] = r_dwm.inc(
                GP.dma_start(out=winT_b[kw][:, :], in_=winT5[et2, :, :, :]), kw)

        def psum_wait(eng, key):
            if key in psum_free:
                sem, val = psum_free[key]
                eng.wait_ge(sem, val)

        def emit_m1_et(p, et):
            gi = p * NE + et
            kw = gi % 3
            k2 = gi % 2
            poff, pln = PASS_OFF[p], PASS_LEN[p]
            chs = PASS_CH[p]
            nch = len(chs)
            if gi + 2 < 2 * NE:
                emit_winT_load(gi + 2)
            TE.wait_ge(c_dthl.h, 2 if p == 0 else NCH)
            if gi == 0:
                AC.wait_ge(e_dve.h, negA_ready)
            if gi >= 2:
                AC.wait_ge(c_scan.h, gi - 1)

            def sel_chunk(bo):
                bl = min(512, pln - bo)
                if p == 0:
                    bkey = ("xp", 4 + bo // 512)
                    bps = ps_xp6[:, (4 + bo // 512) * 512:(4 + bo // 512) * 512 + bl]
                else:
                    bkey = ("dtl", (gi + bo // 512) % 2)
                    bps = ps_dtl[(gi + bo // 512) % 2]
                psum_wait(TE, bkey)
                co = poff + bo
                inst = TE.matmul(bps[:, 0:bl] if p else bps,
                                 sel_sb[0:64, et * P:(et + 1) * P],
                                 dthl_s[0:64, co:co + bl], start=True, stop=True)
                n = c_pexp.inc(inst)
                AC.wait_ge(c_pexp.h, n)
                v = c_exp.inc(AC.activation(out=a_b[k2][:, bo:bo + bl],
                                            in_=bps[:, 0:bl] if p else bps,
                                            func=AF.Exp,
                                            scale=negA_s[:, et:et + 1]))
                psum_free[bkey] = (c_exp.h, v)

            sel_chunk(0)
            sel_chunk(512)
            TE.wait_ge(r_dwm.h[kw], wm_vals[gi])
            for ci, c in enumerate(chs):
                off, ln = CH_OFF[c], CH_LEN[c]
                bank = (et % 2) * (3 if p else 2) + ci
                psum_wait(TE, ("xp", bank))
                for dt in range(ND):
                    inst = TE.matmul(ps_xp6[:, bank * 512: bank * 512 + ln],
                                     winT_b[kw][:, dt * P:(dt + 1) * P],
                                     xnT[:, dt * T + off: dt * T + off + ln],
                                     start=(dt == 0), stop=(dt == ND - 1))
                pexp_at[(gi, ci)] = c_pexp.inc(inst)
                psum_xp_bank[ci] = bank
            pexp_at[gi] = c_pexp.n
            if pln > 1024:
                sel_chunk(1024)
            exp_at[gi] = c_exp.n
            b0 = psum_xp_bank[0]
            AC.wait_ge(c_pexp.h, pexp_at[gi])
            if gi >= 2:
                AC.wait_ge(c_scan.h, gi - 1)
            v = c_evxp.inc(
                AC.activation(out=xps_b[k2][:, 0:pln],
                              in_=ps_xp6[:, b0 * 512: b0 * 512 + pln], func=AF.Copy))
            for ci in range(nch):
                evxp_at[(gi, ci)] = v
                psum_free[("xp", psum_xp_bank[ci])] = (c_evxp.h, v)
            if gi >= 1:
                emit_cast_spill(p if gi != NE else 0, gi - 1)
            VE.wait_ge(c_exp.h, exp_at[gi])
            VE.wait_ge(c_evxp.h, c_evxp.n)
            init = 0.0 if p == 0 else state_s[:, et:et + 1]
            if gi >= 2:
                VE.wait_ge(c_ybf.h, gi - 1)
            inst = VE.tensor_tensor_scan(out=ys_b[k2][:, 0:pln],
                                         data0=a_b[k2][:, 0:pln],
                                         data1=xps_b[k2][:, 0:pln],
                                         initial=init,
                                         op0=ALU.mult, op1=ALU.add)
            c_scan.inc(inst)
            VE.wait_ge(c_scan.h, c_scan.n)
            if p == 0:
                vchain(VE.tensor_copy(out=state_s[:, et:et + 1],
                                      in_=ys_b[k2][:, pln - 1:pln]))

        def emit_cast_spill(p_of, gi2):
            k2 = gi2 % 2
            pln = PASS_LEN[0] if gi2 < NE else PASS_LEN[1]
            p2 = 0 if gi2 < NE else 1
            et2 = gi2 % NE
            AC.wait_ge(c_scan.h, gi2 + 1)
            if gi2 >= 2:
                AC.wait_ge(r_dys.h[k2], r_dys.cnt[k2])
            inst = AC.activation(out=ybf_b[k2][:, 0:pln], in_=ys_b[k2][:, 0:pln],
                                 func=AF.Copy)
            c_ybf.inc(inst)
            GP.wait_ge(c_ybf.h, c_ybf.n)
            if p2 == 0:
                tt0, nblk, col0 = 0, 7, WU
            else:
                tt0, nblk, col0 = 7, 9, 0
            r_dys.refill(GP, k2)
            r_dys.inc(GP.dma_start(out=ys_scr[tt0:tt0 + nblk, :, et2, :]
                                   .rearrange("t p c -> p t c"),
                                   in_=ybf_b[k2][:, col0:col0 + nblk * P]
                                   .rearrange("p (t c) -> p t c", c=P)), k2)

        emit_ln_front()
        emit_ln_front()
        s_dwc.inc(GP.dma_start(out=bdt_c[:, :], in_=bdt_in[:, :]))
        s_dwc.inc(GP.dma_start(out=logA_s[:, :], in_=logA_in[:, :]))
        s_dwc.inc(GP.dma_start(out=sg_c[:, :], in_=sg_in[:, :]))
        s_dwc.inc(GP.dma_start(out=su_c[:, :], in_=su_in[:, :]))
        s_dwc.inc(GP.dma_start(out=wdtT_s[:, :],
                               in_=wdtT3[:, :, :].rearrange("t p c -> p t c")))
        s_dwc.inc(GP.dma_start(out=sel_sb[0:64, :],
                               in_=sel_in[:, :, :].rearrange("t p c -> p t c")))
        n_dwc = s_dwc.n

        vchain(VE.memset(eps_c[:, :], 1e-5))
        vchain(VE.memset(one_c[:, :], 1.0))
        consts_ready[0] = e_dve.n
        AC.wait_ge(s_dwc.h, n_dwc)
        achain(AC.activation(out=negA_s[:, :], in_=logA_s[:, :], func=AF.Exp))
        VE.wait_ge(e_act.h, e_act.n)
        vchain(VE.tensor_scalar_mul(out=negA_s[:, :], in0=negA_s[:, :], scalar1=-1.0))
        negA_ready = e_dve.n

        def emit_wc_step(dt):
            kb = dt % 5
            if dt >= 5:
                GP.wait_ge(c_pewc.h, dt - 4)
            else:
                GP.wait_ge(s_dwc.h, s_dwc.n)
            r_dwn.refill(GP, kb)
            r_dwn.inc(GP.dma_start(out=winN_b[kb][:, :], in_=winN5[dt, :, :, :]), kb)
            r_dwn.wait(TE, kb)
            if dt == 0:
                TE.wait_ge(s_dwc.h, n_dwc)
            if dt >= 4:
                TE.wait_ge(c_evwc.h, dt - 3)
            wb = dt % 4
            wps = (ps_dtl[wb][:, 0:32] if wb < 2
                   else ps_xp6[:, (wb - 2) * 512:(wb - 2) * 512 + 32])
            for et in range(NE):
                inst = TE.matmul(wps, winN_b[kb][:, et * P:(et + 1) * P],
                                 wdtT_s[:, et * 32:(et + 1) * 32],
                                 start=(et == 0), stop=(et == NE - 1))
            c_pewc.inc(inst)
            AC.wait_ge(c_pewc.h, c_pewc.n)
            v = c_evwc.inc(AC.activation(out=wc_sb[:, dt * 32:(dt + 1) * 32],
                                         in_=wps, func=AF.Copy))
            if dt >= 12:
                psum_free[("dtl", wb) if wb < 2 else ("xp", wb - 2)] = (c_evwc.h, v)

        hf = {0: 2, 1: 3, 3: 4, 5: 5, 7: 6, 9: 7, 11: 8, 13: 9}
        hb = {0: 0, 1: 1, 2: 2, 3: 3, 5: 4, 7: 5, 9: 6, 11: 7}
        for r in range(ND):
            if r in hb and r <= 3:
                emit_ln_back(hb[r])
            emit_wc_step(r)
            if r in hb and r > 3:
                emit_ln_back(hb[r])
            if r in hf:
                emit_ln_front()
            if r == 4:
                emit_tr(0)
            if r == 12:
                emit_tr(1)
        emit_dtl(0)
        emit_dtl(1)
        wn_ctx.close()
        emit_winT_load(0)
        emit_winT_load(1)
        fr = {0: 1, 1: 1, 2: 1, 3: 1, 4: 1, 5: 1, 6: 1}
        bk = {0: 8, 1: 9, 2: 10, 3: 11, 4: 12, 5: 13, 6: 14, 7: 15, 8: 16}
        for et in range(NE):
            emit_m1_et(0, et)
            if et in bk:
                emit_ln_back(bk[et])
            if et in fr:
                emit_ln_front()
            if et == 10:
                emit_tr(2)
            elif et == 11:
                emit_dtl(2)
            elif et == 12:
                emit_tr(3)
            elif et == 13:
                emit_dtl(3)
                emit_tr(4)
            elif et == 14:
                emit_dtl(4)
        for et in range(NE):
            emit_m1_et(1, et)
        emit_cast_spill(1, 2 * NE - 1)

        n_pexp_all = c_pexp.n
        pa_ctx.close()
        ab_ctx.close()

        n_pexp_all = c_pexp.n
        pa_ctx.close()
        ab_ctx.close()

        wdq_ctx = contextlib.ExitStack()
        wdq_sb = wdq_ctx.enter_context(nc.sbuf_tensor("wdq_sb", [P, NJ * QW], BF16, side="right"))
        sdq_b = [wdq_ctx.enter_context(nc.sbuf_tensor(f"sdqb{i}", [P, QW], F32,
                                                      side="right"))
                 for i in range(2)]
        yt_b = [wdq_ctx.enter_context(nc.sbuf_tensor(f"ytb{i}", [P, ND * P], BF16,
                                                     side="right"))
                for i in range(3)]
        wg0_sb = wdq_ctx.enter_context(nc.sbuf_tensor("wg0_sb", [P, ND * P], BF16,
                                                      side="right"))
        wu0_sb = wdq_ctx.enter_context(nc.sbuf_tensor("wu0_sb", [P, ND * P], BF16,
                                                      side="right"))

        def emit_wd_group(q, g8):
            if q >= 1:
                GP.wait_ge(c_peo.h, NTT * q)
            r_dwd.refill(GP, g8)
            r_dwd.inc(GP.dma_start(out=wdq_sb[:, g8 * GJ * QW:(g8 + 1) * GJ * QW],
                                   in_=wd5[q, :, g8 * GJ:(g8 + 1) * GJ, :]), g8)
            r_dwd.wait(VE, g8)
            r_dsd.wait(VE, q % 2)
            wv = wdq_sb[:, g8 * GJ * QW:(g8 + 1) * GJ * QW] \
                .rearrange("p (t c) -> p t c", c=QW)
            c_wds.inc(VE.tensor_tensor(out=wv, in0=wv,
                                       in1=sdq_b[q % 2][:, :].unsqueeze(1)
                                       .broadcast_to([P, GJ, QW]),
                                       op=ALU.mult))

        di_ctx = contextlib.ExitStack()
        DE = di_ctx.enter_context
        wg_b = [DE(nc.sbuf_tensor(f"wgb{i}", [P, ND * P], BF16)) for i in range(3)]
        wub_b = [DE(nc.sbuf_tensor(f"wub{i}", [P, ND * P], BF16)) for i in range(3)]
        g_b = [DE(nc.sbuf_tensor(f"gb{i}", [P, TR], BF16)) for i in range(2)]
        u_b = [DE(nc.sbuf_tensor(f"ub{i}", [P, TR], BF16)) for i in range(2)]
        f_b = [DE(nc.sbuf_tensor(f"fb{i}", [P, TR], BF16)) for i in range(2)]

        pdi_ctx = contextlib.ExitStack()
        ps_g = pdi_ctx.enter_context(nc.psum_tensor("ps_g", [P, TR], F32))
        ps_u = pdi_ctx.enter_context(nc.psum_tensor("ps_u", [P, TR], F32))

        def di_loads(j):
            ks = j % 3
            if j >= 3:
                GP.wait_ge(c_pegu.h, 8 * (j - 2))
            r_dwg.refill(GP, ks)
            r_dwg.inc(GP.dma_start(out=wg_b[ks][:, :], in_=wg5[j, :, :, :]), ks)
            r_dwu.refill(GP, ks)
            r_dwu.inc(GP.dma_start(out=wub_b[ks][:, :], in_=wu5[j, :, :, :]), ks)

        SY.wait_ge(c_scan.h, 32)
        r_dsd.refill(SY, 0)
        r_dsd.inc(SY.dma_start(out=sdq_b[0][:, :],
                               in_=sd_in[0:1, 0:QW].broadcast_to([P, QW])), 0)
        r_dwg.refill(GP, 0)
        r_dwg.inc(GP.dma_start(out=wg0_sb[:, :], in_=wg5[0, :, :, :]), 0)
        r_dwu.refill(GP, 0)
        r_dwu.inc(GP.dma_start(out=wu0_sb[:, :], in_=wu5[0, :, :, :]), 0)
        for j in range(NJ):
            kb = j % 3
            kf2 = j % 2
            if j == 0:
                r_dys.wait_all(GP)
                GP.wait_ge(c_scan.h, 32)
                di_loads(1)
            if j == 2:
                r_dys.wait_all(SY)
                r_dyl.refill(SY, 0)
                r_dyl.inc(SY.dma_start(out=yt_b[0][:, :], in_=ys_scr[0, :, :, :]), 0)
            elif j == 4:
                r_dyl.refill(SY, 1)
                r_dyl.inc(SY.dma_start(out=yt_b[1][:, :], in_=ys_scr[1, :, :, :]), 1)
            if j >= NJ - NG:
                emit_wd_group(0, j - (NJ - NG))
            if j + 2 < NJ:
                di_loads(j + 2)
            if j == 0:
                TE.wait_ge(c_evxp.h, c_evxp.n)
                TE.wait_ge(c_z.h, NCH)
                TE.wait_ge(c_scan.h, 32)
            for gu, (ps, wsb, rng) in enumerate(((ps_g, wg_b, r_dwg),
                                                 (ps_u, wub_b, r_dwu))):
                rng.wait(TE, kb)
                wtile = (wg0_sb if gu == 0 else wu0_sb) if j == 0 else wsb[kb]
                for ch in range(4):
                    if j >= 1:
                        TE.wait_ge(c_ev.h, 8 * (j - 1) + gu * 4 + ch + 1)
                    cols = slice(ch * 512, (ch + 1) * 512)
                    for dt in range(ND):
                        inst = TE.matmul(ps[:, cols], wtile[:, dt * P:(dt + 1) * P],
                                         xnT[:, dt * T + WU + ch * 512:
                                             dt * T + WU + ch * 512 + 512],
                                         start=(dt == 0), stop=(dt == ND - 1))
                    c_pegu.inc(inst)
            for gu in range(2):
                for ch in range(4):
                    AC.wait_ge(c_pegu.h, 8 * j + gu * 4 + ch + 1)
                    if j >= 2:
                        AC.wait_ge(c_f.h, 4 * (j - 1))
                    cols = slice(ch * 512, (ch + 1) * 512)
                    if gu == 0:
                        inst = AC.activation(out=g_b[kf2][:, cols], in_=ps_g[:, cols],
                                             func=AF.Silu, scale=sg_c[:, j:j + 1])
                    else:
                        inst = AC.activation(out=u_b[kf2][:, cols], in_=ps_u[:, cols],
                                             func=AF.Identity, scale=su_c[:, j:j + 1])
                    c_ev.inc(inst)
            for ch in range(4):
                VE.wait_ge(c_ev.h, 8 * j + 4 + ch + 1)
                if j >= 2:
                    VE.wait_ge(r_df.h[kf2], r_df.cnt[kf2])
                cols = slice(ch * 512, (ch + 1) * 512)
                c_f.inc(VE.tensor_mul(out=f_b[kf2][:, cols], in0=g_b[kf2][:, cols],
                                      in1=u_b[kf2][:, cols]))
            GP.wait_ge(c_f.h, 4 * (j + 1))
            r_df.refill(GP, kf2)
            r_df.inc(GP.dma_start(out=f_scr[:, :, j, :].rearrange("t p c -> p t c"),
                                  in_=f_b[kf2][:, :].rearrange("p (t c) -> p t c", c=P)),
                     kf2)

        n_pegu_all = c_pegu.n
        n_ev_all = c_ev.n
        pdi_ctx.close()
        di_ctx.close()
        xnT_ctx.close()

        dii_ctx = contextlib.ExitStack()
        FE = dii_ctx.enter_context
        wo_sb = [FE(nc.sbuf_tensor(f"wo_sb{i}", [P, ND * QW], BF16)) for i in range(2)]
        ft_b = [FE(nc.sbuf_tensor(f"ftb{i}", [P, NJ * P], BF16)) for i in range(3)]
        xs_b = [FE(nc.sbuf_tensor(f"xsb{i}", [P, QW], F32)) for i in range(3)]
        ob_b = [FE(nc.sbuf_tensor(f"obb{i}", [P, QW], F32)) for i in range(4)]

        pdii_ctx = contextlib.ExitStack()
        ps_o = [pdii_ctx.enter_context(nc.psum_tensor(f"pso{i}", [P, QW], F32))
                for i in range(4)]

        def dii_loads(q, tt):
            qt = q * NTT + tt
            kf = qt % 3
            if qt >= 3:
                SY.wait_ge(c_peo.h, qt - 2)
            r_dfl.refill(SY, kf)
            r_dfl.inc(SY.dma_start(out=ft_b[kf][:, :], in_=f_scr[tt, :, :, :]), kf)
            if qt >= 2:
                r_dyl.refill(SY, kf)
                r_dyl.inc(SY.dma_start(out=yt_b[kf][:, :], in_=ys_scr[tt, :, :, :]), kf)
            if qt >= 3:
                SY.wait_ge(c_add.h, qt - 2)
            r_dx2.refill(SY, kf)
            r_dx2.inc(SY.dma_start(out=xs_b[kf][:, :],
                                   in_=x_full[WU + tt * P: WU + (tt + 1) * P,
                                              q * QW:(q + 1) * QW]), kf)

        wo_vals = {}
        for q in range(NQ):
            kq = q % 2
            if q + 1 < NQ:
                if q + 1 >= 2:
                    SY.wait_ge(c_wds.h, NG * q)
                r_dsd.refill(SY, (q + 1) % 2)
                r_dsd.inc(SY.dma_start(out=sdq_b[(q + 1) % 2][:, :],
                                       in_=sd_in[0:1, (q + 1) * QW:(q + 2) * QW]
                                       .broadcast_to([P, QW])), (q + 1) % 2)
            if q >= 1:
                for g8 in range(NG):
                    emit_wd_group(q, g8)
            if q == 0:
                GP.wait_ge(c_pegu.h, n_pegu_all)
                r_dwo.refill(GP, 0)
                wo_vals[0] = r_dwo.inc(
                    GP.dma_start(out=wo_sb[0][:, :], in_=woutT5[0, :, :, :]), 0)
                r_df.wait_all(SY)
                r_dys.wait_all(SY)
                dii_loads(0, 0)
                dii_loads(0, 1)

            for tt in range(NTT):
                qt = q * NTT + tt
                kf = qt % 3
                ko = qt % 4
                pf = qt + 2
                if pf < NQ * NTT:
                    dii_loads(pf // NTT, pf % NTT)
                if tt == 2 and q + 1 < NQ:
                    GP.wait_ge(c_peo.h, NTT * q + 1)
                    r_dwo.refill(GP, (q + 1) % 2)
                    wo_vals[q + 1] = r_dwo.inc(
                        GP.dma_start(out=wo_sb[(q + 1) % 2][:, :],
                                     in_=woutT5[q + 1, :, :, :]), (q + 1) % 2)
                r_dfl.wait(TE, kf)
                r_dyl.wait(TE, kf)
                TE.wait_ge(r_dwo.h[q % 2], wo_vals[q])
                if qt == 0:
                    TE.wait_ge(c_ev.h, n_ev_all)
                if qt >= 4:
                    TE.wait_ge(c_add.h, qt - 3)
                if tt == 1:
                    TE.wait_ge(c_wds.h, NG * (q + 1))
                for dt in range(ND):
                    inst = TE.matmul(ps_o[ko][:, :], yt_b[kf][:, dt * P:(dt + 1) * P],
                                     wo_sb[q % 2][:, dt * QW:(dt + 1) * QW],
                                     start=(dt == 0), stop=False)
                for jt in range(NJ):
                    if tt == 0 and jt % GJ == 0:
                        TE.wait_ge(c_wds.h, NG * q + jt // GJ + 1)
                    inst = TE.matmul(ps_o[ko][:, :], ft_b[kf][:, jt * P:(jt + 1) * P],
                                     wdq_sb[:, jt * QW:(jt + 1) * QW],
                                     start=False, stop=(jt == NJ - 1))
                c_peo.inc(inst)
                VE.wait_ge(c_peo.h, c_peo.n)
                r_dx2.wait(VE, kf)
                if qt >= 4:
                    VE.wait_ge(r_dout.h[ko], r_dout.cnt[ko])
                c_add.inc(VE.tensor_add(out=ob_b[ko][:, :], in0=ps_o[ko][:, :],
                                        in1=xs_b[kf][:, :]))
                SY.wait_ge(c_add.h, c_add.n)
                r_dout.refill(SY, ko)
                r_dout.inc(SY.dma_start(out=y_out[tt * P:(tt + 1) * P,
                                                  q * QW:(q + 1) * QW],
                                        in_=ob_b[ko][:, :]), ko)

        r_dout.wait_all(SY)
        pdii_ctx.close()
        dii_ctx.close()
        wdq_ctx.close()

    return nc


_NC_CACHE = None


def _get_nc():
    global _NC_CACHE
    if _NC_CACHE is None:
        _NC_CACHE = build_nc()
    return _NC_CACHE


def kernel(L0_attn_x, L0_attn_W_in, L0_attn_W_out, L0_attn_log_A, L0_attn_W_dt,
           L0_attn_b_dt, L0_ffn_w_gate, L0_ffn_w_up, L0_ffn_w_down, L0_ffn_s_gate,
           L0_ffn_s_up, L0_ffn_s_down):
    x = np.ascontiguousarray(np.asarray(L0_attn_x, dtype=np.float32))
    W_in = np.asarray(L0_attn_W_in, dtype=np.float32)
    W_out = np.asarray(L0_attn_W_out, dtype=np.float32)
    log_A = np.asarray(L0_attn_log_A, dtype=np.float32)
    W_dt = np.asarray(L0_attn_W_dt, dtype=np.float32)
    b_dt = np.asarray(L0_attn_b_dt, dtype=np.float32)
    w_gate = np.asarray(L0_ffn_w_gate)
    w_up = np.asarray(L0_ffn_w_up)
    w_down = np.asarray(L0_ffn_w_down)
    s_gate = np.asarray(L0_ffn_s_gate, dtype=np.float32)
    s_up = np.asarray(L0_ffn_s_up, dtype=np.float32)
    s_down = np.asarray(L0_ffn_s_down, dtype=np.float32)

    winT5 = np.ascontiguousarray(
        W_in.T.reshape(ND, P, NE, P).transpose(2, 1, 0, 3))
    winN5 = np.ascontiguousarray(
        W_in.reshape(NE, P, ND, P).transpose(2, 1, 0, 3))
    wdtT3 = np.ascontiguousarray(W_dt.T.reshape(NE, P, 32))
    bdt_c = np.ascontiguousarray(b_dt.reshape(32, 1))
    logA_col = np.ascontiguousarray(log_A.reshape(NE, P).T)
    woutT5 = np.ascontiguousarray(
        W_out.T.reshape(ND, P, NQ, QW).transpose(2, 1, 0, 3))
    wg5 = np.ascontiguousarray(
        w_gate.reshape(ND, P, NJ, P).transpose(2, 1, 0, 3))
    wu5 = np.ascontiguousarray(
        w_up.reshape(ND, P, NJ, P).transpose(2, 1, 0, 3))
    wd5 = np.ascontiguousarray(
        w_down.reshape(NJ, P, NQ, QW).transpose(2, 1, 0, 3))
    sg_col = np.ascontiguousarray(s_gate.reshape(NJ, P).T)
    su_col = np.ascontiguousarray(s_up.reshape(NJ, P).T)
    sd_row = np.ascontiguousarray(s_down.reshape(1, DM))
    sel = np.zeros((NE, 64, P), ml_dtypes.bfloat16)
    for et in range(NE):
        for p_ in range(P):
            sel[et, 2 * et + p_ // 64, p_] = 1.0
            sel[et, 32 + 2 * et + p_ // 64, p_] = 1.0

    shared = dict(winT5=winT5, winN5=winN5, wdtT3=wdtT3, bdt_in=bdt_c,
                  logA_in=logA_col, woutT5=woutT5, wg5=wg5, wu5=wu5, wd5=wd5,
                  sg_in=sg_col, su_in=su_col, sd_in=sd_row, sel_in=sel)

    in_maps = []
    B = x.shape[0]
    for b in range(B):
        for half in range(2):
            start = half * TR
            if start >= WU:
                xf = x[b, start - WU:start + TR, :]
            else:
                xf = np.concatenate(
                    [np.zeros((WU, DM), np.float32), x[b, 0:start + TR, :]], axis=0)
            m = dict(shared)
            m["x_full"] = np.ascontiguousarray(xf)
            in_maps.append(m)

    nc = _get_nc()
    res = run_bass_kernel_spmd(nc, in_maps, core_ids=list(range(8)))

    out = np.empty((B, 2 * TR, DM), np.float32)
    for b in range(B):
        for half in range(2):
            out[b, half * TR:(half + 1) * TR, :] = res.results[2 * b + half]["y_out"]
    return out


# version 26
# speedup vs baseline: 1.0073x; 1.0073x over previous
import contextlib

import numpy as np
import ml_dtypes

import concourse.bass as bass
import concourse.mybir as mybir
from concourse.bass_utils import run_bass_kernel_spmd

F32 = mybir.dt.float32
BF16 = mybir.dt.bfloat16
F32R = mybir.dt.float32r
I8 = mybir.dt.int8
AF = mybir.ActivationFunctionType
ALU = mybir.AluOpType
AX = mybir.AxisListType.X

P = 128
DM = 2048
NE = 16
ND = 16
NJ = 64
WU = 128
TR = 2048
T = WU + TR
CH_OFF = (0, 512, 1024, 1536, 2048)
CH_LEN = (512, 512, 512, 512, 128)
NCH = 5
NQ = 4
QW = 512
NTT = 16
INV_D = 1.0 / DM
NG = 8
GJ = NJ // NG


class Ct:

    def __init__(self, ctx, nc, name, step=1):
        self.h = ctx.enter_context(nc.semaphore(name))
        self.n = 0
        self.step = step

    def inc(self, inst):
        inst.then_inc(self.h, self.step)
        self.n += self.step
        return self.n


class Ring:

    def __init__(self, ctx, nc, name, nslots):
        self.h = [ctx.enter_context(nc.semaphore(f"{name}{i}"))
                  for i in range(nslots)]
        self.cnt = [0] * nslots

    def inc(self, inst, slot=0):
        inst.then_inc(self.h[slot], 16)
        self.cnt[slot] += 16
        return self.cnt[slot]

    def wait(self, eng, slot=0):
        eng.wait_ge(self.h[slot], self.cnt[slot])

    def refill(self, eng, slot=0):
        if self.cnt[slot]:
            eng.wait_ge(self.h[slot], self.cnt[slot])

    def wait_all(self, eng):
        for i, h in enumerate(self.h):
            if self.cnt[i]:
                eng.wait_ge(h, self.cnt[i])


def build_nc():
    nc = bass.Bass("TRN2", target_bir_lowering=False, debug=False)

    x_full = nc.dram_tensor("x_full", [T, DM], F32, kind="ExternalInput").ap()
    winT5 = nc.dram_tensor("winT5", [NE, P, ND, P], F32, kind="ExternalInput").ap()
    winN5 = nc.dram_tensor("winN5", [ND, P, NE, P], F32, kind="ExternalInput").ap()
    wdtT3 = nc.dram_tensor("wdtT3", [NE, P, 32], F32, kind="ExternalInput").ap()
    bdt_in = nc.dram_tensor("bdt_in", [32, 1], F32, kind="ExternalInput").ap()
    logA_in = nc.dram_tensor("logA_in", [P, NE], F32, kind="ExternalInput").ap()
    woutT5 = nc.dram_tensor("woutT5", [NQ, P, ND, QW], F32, kind="ExternalInput").ap()
    wg5 = nc.dram_tensor("wg5", [NJ, P, ND, P], I8, kind="ExternalInput").ap()
    wu5 = nc.dram_tensor("wu5", [NJ, P, ND, P], I8, kind="ExternalInput").ap()
    wd5 = nc.dram_tensor("wd5", [NQ, P, NJ, QW], I8, kind="ExternalInput").ap()
    sg_in = nc.dram_tensor("sg_in", [P, NJ], F32, kind="ExternalInput").ap()
    su_in = nc.dram_tensor("su_in", [P, NJ], F32, kind="ExternalInput").ap()
    sd_in = nc.dram_tensor("sd_in", [1, DM], F32, kind="ExternalInput").ap()
    sel_in = nc.dram_tensor("sel_in", [NE, 64, P], BF16, kind="ExternalInput").ap()
    y_out = nc.dram_tensor("y_out", [TR, DM], F32, kind="ExternalOutput").ap()

    xn_scr = nc.dram_tensor("xn_scr", [T, DM], BF16, kind="Internal").ap()
    ys_scr = nc.dram_tensor("ys_scr", [NTT, P, NE, P], BF16, kind="Internal").ap()
    f_scr = nc.dram_tensor("f_scr", [NTT, P, NJ, P], BF16, kind="Internal").ap()

    SY, VE, AC, TE, GP = nc.sync, nc.vector, nc.scalar, nc.tensor, nc.gpsimd

    with contextlib.ExitStack() as top:
        E = top.enter_context

        r_dx = Ring(top, nc, "r_dx", 3)
        r_dxn = Ring(top, nc, "r_dxn", 3)
        r_dtr = Ring(top, nc, "r_dtr", 2)
        r_dwm = Ring(top, nc, "r_dwm", 3)
        s_dwc = Ct(top, nc, "s_dwc", 16)
        r_dwn = Ring(top, nc, "r_dwn", 5)
        r_dys = Ring(top, nc, "r_dys", 2)
        r_dwg = Ring(top, nc, "r_dwg", 3)
        r_dwu = Ring(top, nc, "r_dwu", 3)
        r_df = Ring(top, nc, "r_df", 2)
        r_dfl = Ring(top, nc, "r_dfl", 3)
        r_dyl = Ring(top, nc, "r_dyl", 3)
        r_dwd = Ring(top, nc, "r_dwd", NG)
        r_dwo = Ring(top, nc, "r_dwo", 2)
        r_dsd = Ring(top, nc, "r_dsd", 2)
        r_dx2 = Ring(top, nc, "r_dx2", 3)
        r_dout = Ring(top, nc, "r_dout", 4)
        c_pewc = Ct(top, nc, "c_pewc")
        c_evwc = Ct(top, nc, "c_evwc")
        c_lnv = Ct(top, nc, "c_lnv")
        c_xn = Ct(top, nc, "c_xn")
        c_pexp = Ct(top, nc, "c_pexp")
        c_evxp = Ct(top, nc, "c_evxp")
        c_pedtl = Ct(top, nc, "c_pedtl")
        c_z = Ct(top, nc, "c_z")
        c_dtc = Ct(top, nc, "c_dtc")
        c_dthl = Ct(top, nc, "c_dthl")
        c_exp = Ct(top, nc, "c_exp")
        c_scan = Ct(top, nc, "c_scan")
        c_ybf = Ct(top, nc, "c_ybf")
        c_pegu = Ct(top, nc, "c_pegu")
        c_ev = Ct(top, nc, "c_ev")
        c_f = Ct(top, nc, "c_f")
        c_peo = Ct(top, nc, "c_peo")
        c_add = Ct(top, nc, "c_add")
        c_wds = Ct(top, nc, "c_wds")
        e_dve = Ct(top, nc, "e_dve")
        e_act = Ct(top, nc, "e_act")
        e_gp = Ct(top, nc, "e_gp")

        bdt_c = E(nc.sbuf_tensor("bdt_c", [32, 1], F32))
        eps_c = E(nc.sbuf_tensor("eps_c", [P, 1], F32))
        one_c = E(nc.sbuf_tensor("one_c", [P, 1], F32))
        logA_s = E(nc.sbuf_tensor("logA_s", [P, NE], F32))
        negA_s = E(nc.sbuf_tensor("negA_s", [P, NE], F32))
        sg_c = E(nc.sbuf_tensor("sg_c", [P, NJ], F32))
        su_c = E(nc.sbuf_tensor("su_c", [P, NJ], F32))
        wc_sb = E(nc.sbuf_tensor("wc_sb", [P, NE * 32], BF16))
        stats = E(nc.sbuf_tensor("stats", [P, 48], F32))

        blk = E(nc.Block())

        def vchain(inst):
            VE.wait_ge(e_dve.h, e_dve.inc(inst))

        def achain(inst):
            AC.wait_ge(e_act.h, e_act.inc(inst))

        xnT_ctx = contextlib.ExitStack()
        xnT = xnT_ctx.enter_context(nc.sbuf_tensor("xnT", [P, NE * T], BF16))

        ab_ctx = contextlib.ExitStack()
        AE = ab_ctx.enter_context
        winT_b = [AE(nc.sbuf_tensor(f"winTb{i}", [P, ND * P], BF16)) for i in range(3)]
        xt_b = [AE(nc.sbuf_tensor(f"xtb{i}", [P, DM], F32)) for i in range(3)]
        xn_b = [AE(nc.sbuf_tensor(f"xnb{i}", [P, DM], BF16)) for i in range(3)]
        xsq_s = AE(nc.sbuf_tensor("xsq_s", [P, DM], BF16))
        xps_b = [AE(nc.sbuf_tensor(f"xpsb{i}", [P, 1152], F32)) for i in range(2)]
        z_s = AE(nc.sbuf_tensor("z_s", [P, 512], F32))
        zl_s = AE(nc.sbuf_tensor("zl_s", [P, 512], F32))
        zr_s = AE(nc.sbuf_tensor("zr_s", [P, 512], F32))
        dt_b = [AE(nc.sbuf_tensor(f"dtb{i}", [P, 512], F32)) for i in range(2)]
        a_b = [AE(nc.sbuf_tensor(f"ab{i}", [P, 1152], F32)) for i in range(2)]
        dthl_s = AE(nc.sbuf_tensor("dthl_s", [64, T], BF16))
        sel_sb = AE(nc.sbuf_tensor("sel_sb", [64, NE * P], BF16))
        ys_b = [AE(nc.sbuf_tensor(f"ysb{i}", [P, 1152], F32)) for i in range(2)]
        ybf_b = [AE(nc.sbuf_tensor(f"ybfb{i}", [P, 1152], BF16)) for i in range(2)]
        state_s = AE(nc.sbuf_tensor("state_s", [P, NE], F32))

        wn_ctx = contextlib.ExitStack()
        winN_b = [wn_ctx.enter_context(nc.sbuf_tensor(f"winNb{i}", [P, NE * P], BF16))
                  for i in range(5)]
        wdtT_s = wn_ctx.enter_context(nc.sbuf_tensor("wdtT_s", [P, NE * 32], BF16))

        pa_ctx = contextlib.ExitStack()
        ps_xp6 = pa_ctx.enter_context(nc.psum_tensor("ps_xp6", [P, 3072], F32))
        ps_xp = [ps_xp6[:, i * 512:(i + 1) * 512] for i in range(6)]
        ps_dtl = [pa_ctx.enter_context(nc.psum_tensor(f"psdtl{i}", [P, 512], F32))
                  for i in range(2)]

        consts_ready = {0: 0}
        PASS_CH = ((0, 1), (2, 3, 4))
        PASS_OFF = (0, 1024)
        PASS_LEN = (1024, 1152)
        wm_vals = {}
        pexp_at = {}
        exp_at = {}
        psum_free = {}
        psum_xp_bank = {}
        evxp_at = {}
        tile_state = {"i": 0}

        def emit_ln_front():
            i = tile_state["i"]
            tile_state["i"] += 1
            kb = i % 3
            row0 = i * P
            if i >= 3:
                AC.wait_ge(c_xn.h, i - 2)
            r_dx.refill(AC, kb)
            r_dx.inc(AC.dma_start(out=xt_b[kb][:, :],
                                  in_=x_full[row0:row0 + P, :]), kb)
            cb = 12 * (i % 3)
            r_dx.wait(VE, kb)
            if i >= 3:
                VE.wait_ge(c_xn.h, i - 2)
            vchain(VE.reduce_sum(out=stats[:, cb:cb + 1], in_=xt_b[kb][:, :], axis=AX))
            vchain(VE.scalar_tensor_tensor(out=xsq_s[:, :], in0=xt_b[kb][:, :],
                                           scalar=1.0, in1=xt_b[kb][:, :],
                                           op0=ALU.mult, op1=ALU.mult,
                                           accum_out=stats[:, cb + 1:cb + 2]))
            vchain(VE.tensor_scalar_mul(out=stats[:, cb + 2:cb + 3],
                                        in0=stats[:, cb:cb + 1], scalar1=-INV_D))
            vchain(VE.tensor_scalar_mul(out=stats[:, cb + 3:cb + 4],
                                        in0=stats[:, cb + 1:cb + 2], scalar1=INV_D))
            vchain(VE.tensor_mul(out=stats[:, cb + 4:cb + 5],
                                 in0=stats[:, cb + 2:cb + 3],
                                 in1=stats[:, cb + 2:cb + 3]))
            inst = VE.tensor_sub(out=stats[:, cb + 5:cb + 6],
                                 in0=stats[:, cb + 3:cb + 4],
                                 in1=stats[:, cb + 4:cb + 5])
            c_lnv.inc(inst)

        def emit_ln_back(i):
            kb = i % 3
            cb = 12 * (i % 3)
            row0 = i * P
            if i == 0:
                AC.wait_ge(e_dve.h, consts_ready[0])
            AC.wait_ge(c_lnv.h, i + 1)
            achain(AC.activation(out=stats[:, cb + 6:cb + 7],
                                 in_=stats[:, cb + 5:cb + 6], func=AF.Ln,
                                 bias=eps_c[:, :]))
            achain(AC.activation(out=stats[:, cb + 7:cb + 8],
                                 in_=stats[:, cb + 6:cb + 7], func=AF.Exp,
                                 scale=-0.5))
            achain(AC.mul(out=stats[:, cb + 8:cb + 9], in_=stats[:, cb + 2:cb + 3],
                          mul=stats[:, cb + 7:cb + 8]))
            if i >= 3:
                AC.wait_ge(r_dxn.h[kb], r_dxn.cnt[kb])
            inst = AC.activation(out=xn_b[kb][:, :], in_=xt_b[kb][:, :],
                                 func=AF.Identity, scale=stats[:, cb + 7:cb + 8],
                                 bias=stats[:, cb + 8:cb + 9])
            c_xn.inc(inst)
            SY.wait_ge(c_xn.h, c_xn.n)
            r_dxn.refill(SY, kb)
            r_dxn.inc(SY.dma_start(out=xn_scr[row0:row0 + P, :],
                                   in_=xn_b[kb][:, :]), kb)

        def psum_wait_dtl(eng, key_i):
            key = ("dtl", key_i)
            if key in psum_free:
                sem, val = psum_free[key]
                eng.wait_ge(sem, val)

        def emit_tr(c):
            off, ln = CH_OFF[c], CH_LEN[c]
            kc = c % 2
            r_dxn.wait_all(SY)
            if c >= 2:
                SY.wait_ge(c_pedtl.h, c - 1)
            r_dtr.refill(SY, kc)
            for dt in range(ND):
                r_dtr.inc(SY.dma_start(out=xnT[:, dt * T + off: dt * T + off + ln],
                                       in_=xn_scr[off:off + ln, dt * P:(dt + 1) * P],
                                       transpose=True), kc)

        def emit_dtl(c):
            off, ln = CH_OFF[c], CH_LEN[c]
            kc = c % 2
            kdt = c % 2
            r_dtr.wait(TE, kc)
            if c == 0:
                TE.wait_ge(c_evwc.h, ND)
            psum_wait_dtl(TE, kdt)
            for dt in range(ND):
                inst = TE.matmul(ps_dtl[kdt][0:32, 0:ln], wc_sb[:, dt * 32:(dt + 1) * 32],
                                 xnT[:, dt * T + off: dt * T + off + ln],
                                 start=(dt == 0), stop=(dt == ND - 1))
            c_pedtl.inc(inst)
            AC.wait_ge(c_pedtl.h, c_pedtl.n)
            if c >= 1:
                AC.wait_ge(c_dtc.h, c)
            v = c_z.inc(AC.activation(out=z_s[0:32, 0:ln], in_=ps_dtl[kdt][0:32, 0:ln],
                                      func=AF.Identity, bias=bdt_c[0:32, :]))
            psum_free[("dtl", kdt)] = (c_z.h, v)
            AC.wait_ge(c_z.h, c_z.n)
            achain(AC.activation(out=zl_s[0:32, 0:ln], in_=z_s[0:32, 0:ln], func=AF.Abs))
            achain(AC.activation(out=zl_s[0:32, 0:ln], in_=zl_s[0:32, 0:ln],
                                 func=AF.Exp, scale=-1.0))
            achain(AC.activation(out=zl_s[0:32, 0:ln], in_=zl_s[0:32, 0:ln],
                                 func=AF.Ln, bias=one_c[0:32, :]))
            achain(AC.activation(out=zr_s[0:32, 0:ln], in_=z_s[0:32, 0:ln], func=AF.Relu))
            VE.wait_ge(e_act.h, e_act.n)
            inst = VE.tensor_add(out=dt_b[kdt][0:32, 0:ln], in0=zr_s[0:32, 0:ln],
                                 in1=zl_s[0:32, 0:ln])
            c_dtc.inc(inst)
            VE.wait_ge(c_dtc.h, c_dtc.n)
            vchain(VE.tensor_copy(out=dthl_s[0:32, off:off + ln],
                                  in_=dt_b[kdt][0:32, 0:ln]))
            inst = VE.tensor_sub(out=dthl_s[32:64, off:off + ln],
                                 in0=dt_b[kdt][0:32, 0:ln],
                                 in1=dthl_s[0:32, off:off + ln])
            c_dthl.inc(inst)

        def emit_tr_dtl(c):
            emit_tr(c)
            emit_dtl(c)

        def emit_winT_load(gi2):
            et2 = gi2 % NE
            kw = gi2 % 3
            if gi2 >= 3:
                GP.wait_ge(c_pexp.h, pexp_at[gi2 - 3])
            r_dwm.refill(GP, kw)
            wm_vals[gi2] = r_dwm.inc(
                GP.dma_start(out=winT_b[kw][:, :], in_=winT5[et2, :, :, :]), kw)

        def psum_wait(eng, key):
            if key in psum_free:
                sem, val = psum_free[key]
                eng.wait_ge(sem, val)

        def emit_m1_et(p, et):
            gi = p * NE + et
            kw = gi % 3
            k2 = gi % 2
            poff, pln = PASS_OFF[p], PASS_LEN[p]
            chs = PASS_CH[p]
            nch = len(chs)
            if gi + 2 < 2 * NE:
                emit_winT_load(gi + 2)
            TE.wait_ge(c_dthl.h, 2 if p == 0 else NCH)
            if gi == 0:
                AC.wait_ge(e_dve.h, negA_ready)
            if gi >= 2:
                AC.wait_ge(c_scan.h, gi - 1)

            def sel_chunk(bo):
                bl = min(512, pln - bo)
                if p == 0:
                    bkey = ("xp", 4 + bo // 512)
                    bps = ps_xp6[:, (4 + bo // 512) * 512:(4 + bo // 512) * 512 + bl]
                else:
                    bkey = ("dtl", (gi + bo // 512) % 2)
                    bps = ps_dtl[(gi + bo // 512) % 2]
                psum_wait(TE, bkey)
                co = poff + bo
                inst = TE.matmul(bps[:, 0:bl] if p else bps,
                                 sel_sb[0:64, et * P:(et + 1) * P],
                                 dthl_s[0:64, co:co + bl], start=True, stop=True)
                n = c_pexp.inc(inst)
                AC.wait_ge(c_pexp.h, n)
                v = c_exp.inc(AC.activation(out=a_b[k2][:, bo:bo + bl],
                                            in_=bps[:, 0:bl] if p else bps,
                                            func=AF.Exp,
                                            scale=negA_s[:, et:et + 1]))
                psum_free[bkey] = (c_exp.h, v)

            sel_chunk(0)
            sel_chunk(512)
            TE.wait_ge(r_dwm.h[kw], wm_vals[gi])
            for ci, c in enumerate(chs):
                off, ln = CH_OFF[c], CH_LEN[c]
                bank = (et % 2) * (3 if p else 2) + ci
                psum_wait(TE, ("xp", bank))
                for dt in range(ND):
                    inst = TE.matmul(ps_xp6[:, bank * 512: bank * 512 + ln],
                                     winT_b[kw][:, dt * P:(dt + 1) * P],
                                     xnT[:, dt * T + off: dt * T + off + ln],
                                     start=(dt == 0), stop=(dt == ND - 1))
                pexp_at[(gi, ci)] = c_pexp.inc(inst)
                psum_xp_bank[ci] = bank
            pexp_at[gi] = c_pexp.n
            if pln > 1024:
                sel_chunk(1024)
            exp_at[gi] = c_exp.n
            b0 = psum_xp_bank[0]
            AC.wait_ge(c_pexp.h, pexp_at[gi])
            if gi >= 2:
                AC.wait_ge(c_scan.h, gi - 1)
            v = c_evxp.inc(
                AC.activation(out=xps_b[k2][:, 0:pln],
                              in_=ps_xp6[:, b0 * 512: b0 * 512 + pln], func=AF.Copy))
            for ci in range(nch):
                evxp_at[(gi, ci)] = v
                psum_free[("xp", psum_xp_bank[ci])] = (c_evxp.h, v)
            if gi >= 1:
                emit_cast_spill(p if gi != NE else 0, gi - 1)
            VE.wait_ge(c_exp.h, exp_at[gi])
            VE.wait_ge(c_evxp.h, c_evxp.n)
            init = 0.0 if p == 0 else state_s[:, et:et + 1]
            if gi >= 2:
                VE.wait_ge(c_ybf.h, gi - 1)
            inst = VE.tensor_tensor_scan(out=ys_b[k2][:, 0:pln],
                                         data0=a_b[k2][:, 0:pln],
                                         data1=xps_b[k2][:, 0:pln],
                                         initial=init,
                                         op0=ALU.mult, op1=ALU.add)
            c_scan.inc(inst)
            VE.wait_ge(c_scan.h, c_scan.n)
            if p == 0:
                vchain(VE.tensor_copy(out=state_s[:, et:et + 1],
                                      in_=ys_b[k2][:, pln - 1:pln]))

        def emit_cast_spill(p_of, gi2):
            k2 = gi2 % 2
            pln = PASS_LEN[0] if gi2 < NE else PASS_LEN[1]
            p2 = 0 if gi2 < NE else 1
            et2 = gi2 % NE
            AC.wait_ge(c_scan.h, gi2 + 1)
            if gi2 >= 2:
                AC.wait_ge(r_dys.h[k2], r_dys.cnt[k2])
            inst = AC.activation(out=ybf_b[k2][:, 0:pln], in_=ys_b[k2][:, 0:pln],
                                 func=AF.Copy)
            c_ybf.inc(inst)
            GP.wait_ge(c_ybf.h, c_ybf.n)
            if p2 == 0:
                tt0, nblk, col0 = 0, 7, WU
            else:
                tt0, nblk, col0 = 7, 9, 0
            r_dys.refill(GP, k2)
            r_dys.inc(GP.dma_start(out=ys_scr[tt0:tt0 + nblk, :, et2, :]
                                   .rearrange("t p c -> p t c"),
                                   in_=ybf_b[k2][:, col0:col0 + nblk * P]
                                   .rearrange("p (t c) -> p t c", c=P)), k2)

        emit_ln_front()
        emit_ln_front()
        s_dwc.inc(GP.dma_start(out=bdt_c[:, :], in_=bdt_in[:, :]))
        s_dwc.inc(GP.dma_start(out=logA_s[:, :], in_=logA_in[:, :]))
        s_dwc.inc(GP.dma_start(out=sg_c[:, :], in_=sg_in[:, :]))
        s_dwc.inc(GP.dma_start(out=su_c[:, :], in_=su_in[:, :]))
        s_dwc.inc(GP.dma_start(out=wdtT_s[:, :],
                               in_=wdtT3[:, :, :].rearrange("t p c -> p t c")))
        s_dwc.inc(GP.dma_start(out=sel_sb[0:64, :],
                               in_=sel_in[:, :, :].rearrange("t p c -> p t c")))
        n_dwc = s_dwc.n

        vchain(VE.memset(eps_c[:, :], 1e-5))
        vchain(VE.memset(one_c[:, :], 1.0))
        consts_ready[0] = e_dve.n
        AC.wait_ge(s_dwc.h, n_dwc)
        achain(AC.activation(out=negA_s[:, :], in_=logA_s[:, :], func=AF.Exp))
        VE.wait_ge(e_act.h, e_act.n)
        vchain(VE.tensor_scalar_mul(out=negA_s[:, :], in0=negA_s[:, :], scalar1=-1.0))
        negA_ready = e_dve.n

        def emit_wc_step(dt):
            kb = dt % 5
            if dt >= 5:
                GP.wait_ge(c_pewc.h, dt - 4)
            else:
                GP.wait_ge(s_dwc.h, s_dwc.n)
            r_dwn.refill(GP, kb)
            r_dwn.inc(GP.dma_start(out=winN_b[kb][:, :], in_=winN5[dt, :, :, :]), kb)
            r_dwn.wait(TE, kb)
            if dt == 0:
                TE.wait_ge(s_dwc.h, n_dwc)
            if dt >= 2:
                TE.wait_ge(c_evwc.h, dt - 1)
            for et in range(NE):
                inst = TE.matmul(ps_dtl[dt % 2][:, 0:32], winN_b[kb][:, et * P:(et + 1) * P],
                                 wdtT_s[:, et * 32:(et + 1) * 32],
                                 start=(et == 0), stop=(et == NE - 1))
            c_pewc.inc(inst)
            AC.wait_ge(c_pewc.h, c_pewc.n)
            c_evwc.inc(AC.activation(out=wc_sb[:, dt * 32:(dt + 1) * 32],
                                     in_=ps_dtl[dt % 2][:, 0:32], func=AF.Copy))

        hf = {0: 2, 1: 3, 3: 4, 5: 5, 7: 6, 9: 7, 11: 8, 13: 9}
        hb = {1: 0, 3: 1, 5: 2, 7: 3, 9: 4, 11: 5, 12: 6, 13: 7}
        for r in range(ND):
            emit_wc_step(r)
            if r in hb:
                emit_ln_back(hb[r])
            if r in hf:
                emit_ln_front()
            if r == 8:
                emit_tr(0)
            if r == 14:
                emit_tr(1)
        emit_dtl(0)
        emit_dtl(1)
        wn_ctx.close()
        emit_winT_load(0)
        emit_winT_load(1)
        fr = {0: 1, 1: 1, 2: 1, 3: 1, 4: 1, 5: 1, 6: 1}
        bk = {0: 8, 1: 9, 2: 10, 3: 11, 4: 12, 5: 13, 6: 14, 7: 15, 8: 16}
        for et in range(NE):
            emit_m1_et(0, et)
            if et in bk:
                emit_ln_back(bk[et])
            if et in fr:
                emit_ln_front()
            if et == 10:
                emit_tr(2)
            elif et == 11:
                emit_dtl(2)
            elif et == 12:
                emit_tr(3)
            elif et == 13:
                emit_dtl(3)
                emit_tr(4)
            elif et == 14:
                emit_dtl(4)
        for et in range(NE):
            emit_m1_et(1, et)
        emit_cast_spill(1, 2 * NE - 1)

        n_pexp_all = c_pexp.n
        pa_ctx.close()
        ab_ctx.close()

        n_pexp_all = c_pexp.n
        pa_ctx.close()
        ab_ctx.close()

        wdq_ctx = contextlib.ExitStack()
        wdq_sb = wdq_ctx.enter_context(nc.sbuf_tensor("wdq_sb", [P, NJ * QW], BF16, side="right"))
        sdq_b = [wdq_ctx.enter_context(nc.sbuf_tensor(f"sdqb{i}", [P, QW], F32,
                                                      side="right"))
                 for i in range(2)]
        yt_b = [wdq_ctx.enter_context(nc.sbuf_tensor(f"ytb{i}", [P, ND * P], BF16,
                                                     side="right"))
                for i in range(3)]
        wo_sb0 = wdq_ctx.enter_context(nc.sbuf_tensor("wo_sb0", [P, 8 * QW], BF16,
                                                      side="right"))

        def emit_wd_group(q, g8):
            if q >= 1:
                GP.wait_ge(c_peo.h, NTT * q)
            r_dwd.refill(GP, g8)
            r_dwd.inc(GP.dma_start(out=wdq_sb[:, g8 * GJ * QW:(g8 + 1) * GJ * QW],
                                   in_=wd5[q, :, g8 * GJ:(g8 + 1) * GJ, :]), g8)
            r_dwd.wait(VE, g8)
            r_dsd.wait(VE, q % 2)
            wv = wdq_sb[:, g8 * GJ * QW:(g8 + 1) * GJ * QW] \
                .rearrange("p (t c) -> p t c", c=QW)
            c_wds.inc(VE.tensor_tensor(out=wv, in0=wv,
                                       in1=sdq_b[q % 2][:, :].unsqueeze(1)
                                       .broadcast_to([P, GJ, QW]),
                                       op=ALU.mult))

        di_ctx = contextlib.ExitStack()
        DE = di_ctx.enter_context
        wg_b = [DE(nc.sbuf_tensor(f"wgb{i}", [P, ND * P], BF16)) for i in range(3)]
        wub_b = [DE(nc.sbuf_tensor(f"wub{i}", [P, ND * P], BF16)) for i in range(3)]
        g_b = [DE(nc.sbuf_tensor(f"gb{i}", [P, TR], BF16)) for i in range(2)]
        u_b = [DE(nc.sbuf_tensor(f"ub{i}", [P, TR], BF16)) for i in range(2)]
        f_b = [DE(nc.sbuf_tensor(f"fb{i}", [P, TR], BF16)) for i in range(2)]

        pdi_ctx = contextlib.ExitStack()
        ps_g = pdi_ctx.enter_context(nc.psum_tensor("ps_g", [P, TR], F32))
        ps_u = pdi_ctx.enter_context(nc.psum_tensor("ps_u", [P, TR], F32))

        def di_loads(j):
            ks = j % 3
            if j >= 3:
                GP.wait_ge(c_pegu.h, 8 * (j - 2))
            r_dwg.refill(GP, ks)
            r_dwg.inc(GP.dma_start(out=wg_b[ks][:, :], in_=wg5[j, :, :, :]), ks)
            r_dwu.refill(GP, ks)
            r_dwu.inc(GP.dma_start(out=wub_b[ks][:, :], in_=wu5[j, :, :, :]), ks)

        SY.wait_ge(c_scan.h, 32)
        r_dsd.refill(SY, 0)
        r_dsd.inc(SY.dma_start(out=sdq_b[0][:, :],
                               in_=sd_in[0:1, 0:QW].broadcast_to([P, QW])), 0)
        wo_vals = {}
        for j in range(NJ):
            kb = j % 3
            kf2 = j % 2
            if j == 0:
                r_dys.wait_all(GP)
                GP.wait_ge(c_scan.h, 32)
                di_loads(0)
                di_loads(1)
            if j == 2:
                r_dys.wait_all(SY)
                r_dyl.refill(SY, 0)
                r_dyl.inc(SY.dma_start(out=yt_b[0][:, :], in_=ys_scr[0, :, :, :]), 0)
            elif j == 4:
                r_dyl.refill(SY, 1)
                r_dyl.inc(SY.dma_start(out=yt_b[1][:, :], in_=ys_scr[1, :, :, :]), 1)
            if j == 48:
                r_dwo.refill(GP, 0)
                wo_early = r_dwo.inc(
                    GP.dma_start(out=wo_sb0[:, :], in_=woutT5[0, :, 0:8, :]), 0)
            if j >= NJ - NG:
                emit_wd_group(0, j - (NJ - NG))
            if j + 2 < NJ:
                di_loads(j + 2)
            if j == 0:
                TE.wait_ge(c_evxp.h, c_evxp.n)
                TE.wait_ge(c_z.h, NCH)
                TE.wait_ge(c_scan.h, 32)
            for gu, (ps, wsb, rng) in enumerate(((ps_g, wg_b, r_dwg),
                                                 (ps_u, wub_b, r_dwu))):
                rng.wait(TE, kb)
                wtile = wsb[kb]
                for ch in range(4):
                    if j >= 1:
                        TE.wait_ge(c_ev.h, 8 * (j - 1) + gu * 4 + ch + 1)
                    cols = slice(ch * 512, (ch + 1) * 512)
                    for dt in range(ND):
                        inst = TE.matmul(ps[:, cols], wtile[:, dt * P:(dt + 1) * P],
                                         xnT[:, dt * T + WU + ch * 512:
                                             dt * T + WU + ch * 512 + 512],
                                         start=(dt == 0), stop=(dt == ND - 1))
                    c_pegu.inc(inst)
            for gu in range(2):
                for ch in range(4):
                    AC.wait_ge(c_pegu.h, 8 * j + gu * 4 + ch + 1)
                    if j >= 2:
                        AC.wait_ge(c_f.h, 4 * (j - 1))
                    cols = slice(ch * 512, (ch + 1) * 512)
                    if gu == 0:
                        inst = AC.activation(out=g_b[kf2][:, cols], in_=ps_g[:, cols],
                                             func=AF.Silu, scale=sg_c[:, j:j + 1])
                    else:
                        inst = AC.activation(out=u_b[kf2][:, cols], in_=ps_u[:, cols],
                                             func=AF.Identity, scale=su_c[:, j:j + 1])
                    c_ev.inc(inst)
            for ch in range(4):
                VE.wait_ge(c_ev.h, 8 * j + 4 + ch + 1)
                if j >= 2:
                    VE.wait_ge(r_df.h[kf2], r_df.cnt[kf2])
                cols = slice(ch * 512, (ch + 1) * 512)
                c_f.inc(VE.tensor_mul(out=f_b[kf2][:, cols], in0=g_b[kf2][:, cols],
                                      in1=u_b[kf2][:, cols]))
            GP.wait_ge(c_f.h, 4 * (j + 1))
            r_df.refill(GP, kf2)
            r_df.inc(GP.dma_start(out=f_scr[:, :, j, :].rearrange("t p c -> p t c"),
                                  in_=f_b[kf2][:, :].rearrange("p (t c) -> p t c", c=P)),
                     kf2)

        n_pegu_all = c_pegu.n
        n_ev_all = c_ev.n
        pdi_ctx.close()
        di_ctx.close()
        xnT_ctx.close()

        dii_ctx = contextlib.ExitStack()
        FE = dii_ctx.enter_context
        wo0b = FE(nc.sbuf_tensor("wo0b", [P, ND * QW], BF16))
        wo_sb = [wo0b, FE(nc.sbuf_tensor("wo_sb1", [P, ND * QW], BF16))]
        ft_b = [FE(nc.sbuf_tensor(f"ftb{i}", [P, NJ * P], BF16)) for i in range(3)]
        xs_b = [FE(nc.sbuf_tensor(f"xsb{i}", [P, QW], F32)) for i in range(3)]
        ob_b = [FE(nc.sbuf_tensor(f"obb{i}", [P, QW], F32)) for i in range(4)]

        pdii_ctx = contextlib.ExitStack()
        ps_o = [pdii_ctx.enter_context(nc.psum_tensor(f"pso{i}", [P, QW], F32))
                for i in range(4)]

        def dii_loads(q, tt):
            qt = q * NTT + tt
            kf = qt % 3
            if qt >= 3:
                SY.wait_ge(c_peo.h, qt - 2)
            r_dfl.refill(SY, kf)
            r_dfl.inc(SY.dma_start(out=ft_b[kf][:, :], in_=f_scr[tt, :, :, :]), kf)
            if qt >= 2:
                r_dyl.refill(SY, kf)
                r_dyl.inc(SY.dma_start(out=yt_b[kf][:, :], in_=ys_scr[tt, :, :, :]), kf)
            if qt >= 3:
                SY.wait_ge(c_add.h, qt - 2)
            r_dx2.refill(SY, kf)
            r_dx2.inc(SY.dma_start(out=xs_b[kf][:, :],
                                   in_=x_full[WU + tt * P: WU + (tt + 1) * P,
                                              q * QW:(q + 1) * QW]), kf)

        for q in range(NQ):
            kq = q % 2
            if q + 1 < NQ:
                if q + 1 >= 2:
                    SY.wait_ge(c_wds.h, NG * q)
                r_dsd.refill(SY, (q + 1) % 2)
                r_dsd.inc(SY.dma_start(out=sdq_b[(q + 1) % 2][:, :],
                                       in_=sd_in[0:1, (q + 1) * QW:(q + 2) * QW]
                                       .broadcast_to([P, QW])), (q + 1) % 2)
            if q >= 1:
                for g8 in range(NG):
                    emit_wd_group(q, g8)
            if q == 0:
                GP.wait_ge(c_pegu.h, n_pegu_all)
                r_dwo.refill(GP, 0)
                wo_vals[0] = r_dwo.inc(
                    GP.dma_start(out=wo0b[:, 8 * QW:], in_=woutT5[0, :, 8:ND, :]), 0)
                r_df.wait_all(SY)
                r_dys.wait_all(SY)
                dii_loads(0, 0)
                dii_loads(0, 1)

            for tt in range(NTT):
                qt = q * NTT + tt
                kf = qt % 3
                ko = qt % 4
                pf = qt + 2
                if pf < NQ * NTT:
                    dii_loads(pf // NTT, pf % NTT)
                if tt == 2 and q + 1 < NQ:
                    GP.wait_ge(c_peo.h, NTT * q + 1)
                    r_dwo.refill(GP, (q + 1) % 2)
                    wo_vals[q + 1] = r_dwo.inc(
                        GP.dma_start(out=wo_sb[(q + 1) % 2][:, :],
                                     in_=woutT5[q + 1, :, :, :]), (q + 1) % 2)
                r_dfl.wait(TE, kf)
                r_dyl.wait(TE, kf)
                if q == 0:
                    TE.wait_ge(r_dwo.h[0], wo_early)
                else:
                    TE.wait_ge(r_dwo.h[q % 2], wo_vals[q])
                if qt == 0:
                    TE.wait_ge(c_ev.h, n_ev_all)
                if qt >= 4:
                    TE.wait_ge(c_add.h, qt - 3)
                if tt == 1:
                    TE.wait_ge(c_wds.h, NG * (q + 1))
                for dt in range(ND):
                    if q == 0 and dt == 8:
                        TE.wait_ge(r_dwo.h[0], wo_vals[0])
                    wsrc = wo_sb0 if (q == 0 and dt < 8) else wo_sb[q % 2]
                    inst = TE.matmul(ps_o[ko][:, :], yt_b[kf][:, dt * P:(dt + 1) * P],
                                     wsrc[:, dt * QW:(dt + 1) * QW],
                                     start=(dt == 0), stop=False)
                for jt in range(NJ):
                    if tt == 0 and jt % GJ == 0:
                        TE.wait_ge(c_wds.h, NG * q + jt // GJ + 1)
                    inst = TE.matmul(ps_o[ko][:, :], ft_b[kf][:, jt * P:(jt + 1) * P],
                                     wdq_sb[:, jt * QW:(jt + 1) * QW],
                                     start=False, stop=(jt == NJ - 1))
                c_peo.inc(inst)
                VE.wait_ge(c_peo.h, c_peo.n)
                r_dx2.wait(VE, kf)
                if qt >= 4:
                    VE.wait_ge(r_dout.h[ko], r_dout.cnt[ko])
                c_add.inc(VE.tensor_add(out=ob_b[ko][:, :], in0=ps_o[ko][:, :],
                                        in1=xs_b[kf][:, :]))
                SY.wait_ge(c_add.h, c_add.n)
                r_dout.refill(SY, ko)
                r_dout.inc(SY.dma_start(out=y_out[tt * P:(tt + 1) * P,
                                                  q * QW:(q + 1) * QW],
                                        in_=ob_b[ko][:, :]), ko)

        r_dout.wait_all(SY)
        pdii_ctx.close()
        dii_ctx.close()
        wdq_ctx.close()

    return nc


_NC_CACHE = None


def _get_nc():
    global _NC_CACHE
    if _NC_CACHE is None:
        _NC_CACHE = build_nc()
    return _NC_CACHE


def kernel(L0_attn_x, L0_attn_W_in, L0_attn_W_out, L0_attn_log_A, L0_attn_W_dt,
           L0_attn_b_dt, L0_ffn_w_gate, L0_ffn_w_up, L0_ffn_w_down, L0_ffn_s_gate,
           L0_ffn_s_up, L0_ffn_s_down):
    x = np.ascontiguousarray(np.asarray(L0_attn_x, dtype=np.float32))
    W_in = np.asarray(L0_attn_W_in, dtype=np.float32)
    W_out = np.asarray(L0_attn_W_out, dtype=np.float32)
    log_A = np.asarray(L0_attn_log_A, dtype=np.float32)
    W_dt = np.asarray(L0_attn_W_dt, dtype=np.float32)
    b_dt = np.asarray(L0_attn_b_dt, dtype=np.float32)
    w_gate = np.asarray(L0_ffn_w_gate)
    w_up = np.asarray(L0_ffn_w_up)
    w_down = np.asarray(L0_ffn_w_down)
    s_gate = np.asarray(L0_ffn_s_gate, dtype=np.float32)
    s_up = np.asarray(L0_ffn_s_up, dtype=np.float32)
    s_down = np.asarray(L0_ffn_s_down, dtype=np.float32)

    winT5 = np.ascontiguousarray(
        W_in.T.reshape(ND, P, NE, P).transpose(2, 1, 0, 3))
    winN5 = np.ascontiguousarray(
        W_in.reshape(NE, P, ND, P).transpose(2, 1, 0, 3))
    wdtT3 = np.ascontiguousarray(W_dt.T.reshape(NE, P, 32))
    bdt_c = np.ascontiguousarray(b_dt.reshape(32, 1))
    logA_col = np.ascontiguousarray(log_A.reshape(NE, P).T)
    woutT5 = np.ascontiguousarray(
        W_out.T.reshape(ND, P, NQ, QW).transpose(2, 1, 0, 3))
    wg5 = np.ascontiguousarray(
        w_gate.reshape(ND, P, NJ, P).transpose(2, 1, 0, 3))
    wu5 = np.ascontiguousarray(
        w_up.reshape(ND, P, NJ, P).transpose(2, 1, 0, 3))
    wd5 = np.ascontiguousarray(
        w_down.reshape(NJ, P, NQ, QW).transpose(2, 1, 0, 3))
    sg_col = np.ascontiguousarray(s_gate.reshape(NJ, P).T)
    su_col = np.ascontiguousarray(s_up.reshape(NJ, P).T)
    sd_row = np.ascontiguousarray(s_down.reshape(1, DM))
    sel = np.zeros((NE, 64, P), ml_dtypes.bfloat16)
    for et in range(NE):
        for p_ in range(P):
            sel[et, 2 * et + p_ // 64, p_] = 1.0
            sel[et, 32 + 2 * et + p_ // 64, p_] = 1.0

    shared = dict(winT5=winT5, winN5=winN5, wdtT3=wdtT3, bdt_in=bdt_c,
                  logA_in=logA_col, woutT5=woutT5, wg5=wg5, wu5=wu5, wd5=wd5,
                  sg_in=sg_col, su_in=su_col, sd_in=sd_row, sel_in=sel)

    in_maps = []
    B = x.shape[0]
    for b in range(B):
        for half in range(2):
            start = half * TR
            if start >= WU:
                xf = x[b, start - WU:start + TR, :]
            else:
                xf = np.concatenate(
                    [np.zeros((WU, DM), np.float32), x[b, 0:start + TR, :]], axis=0)
            m = dict(shared)
            m["x_full"] = np.ascontiguousarray(xf)
            in_maps.append(m)

    nc = _get_nc()
    res = run_bass_kernel_spmd(nc, in_maps, core_ids=list(range(8)))

    out = np.empty((B, 2 * TR, DM), np.float32)
    for b in range(B):
        for half in range(2):
            out[b, half * TR:(half + 1) * TR, :] = res.results[2 * b + half]["y_out"]
    return out
